# version 45
# baseline (speedup 1.0000x reference)
import contextlib
import numpy as np

import concourse.bass as bass
import concourse.mybir as mybir
from concourse.bass_utils import run_bass_kernel_spmd

F32 = mybir.dt.float32
BF16 = mybir.dt.bfloat16
I8 = mybir.dt.int8
ALU = mybir.AluOpType
ACT = mybir.ActivationFunctionType
AX = mybir.AxisListType.X

L = 2048
D = 1024
H = 16
DH = 64
C = 128
NT = L // C
TPB = 2
NB = NT // TPB
BT = TPB * C
MAGIC = float(np.float32(12582912.0))
EPSF = 1e-6
LNEPS = 1e-5


class Sem:
    def __init__(self, ctx, nc, name):
        self.h = ctx.enter_context(nc.semaphore(name))
        self.n = 0

    def bump(self, k=1):
        self.n += k
        return self.n


class Prog:
    ENGINES = ("gpsimd", "sync", "tensor", "vector", "scalar")

    def __init__(self):
        self.ops = {e: [] for e in self.ENGINES}

    def emit(self, eng, fn):
        self.ops[eng].append(fn)

    def wait(self, eng, tok):
        if tok is None:
            return
        sem, val = tok
        self.ops[eng].append(lambda e, s=sem, v=val: e.wait_ge(s.h, v))

    def waits(self, eng, toks):
        best = {}
        for t in toks:
            if t is None:
                continue
            s, v = t
            if s not in best or v > best[s]:
                best[s] = v
        for s, v in best.items():
            self.ops[eng].append(lambda e, s=s, v=v: e.wait_ge(s.h, v))


def build_nc(debug_taps=False):
    nc = bass.Bass("TRN2", target_bir_lowering=False)

    x_d = nc.dram_tensor("x", [L, D], F32, kind="ExternalInput")
    qkvw_d = nc.dram_tensor("qkvw", [3 * D, D], F32, kind="ExternalInput")
    qkvb_d = nc.dram_tensor("qkvb", [1, 3 * D], F32, kind="ExternalInput")
    outw_d = nc.dram_tensor("outw", [D, D], F32, kind="ExternalInput")
    outb_d = nc.dram_tensor("outb", [1, D], F32, kind="ExternalInput")
    w8_d = nc.dram_tensor("w8", [D, D], I8, kind="ExternalInput")
    ws_d = nc.dram_tensor("ws", [D, 1], F32, kind="ExternalInput")
    bffn_d = nc.dram_tensor("bffn", [1, D], F32, kind="ExternalInput")
    out_d = nc.dram_tensor("out", [L, D], F32, kind="ExternalOutput")

    cc_in = nc.dram_tensor("cc_in", [1, 128], F32)
    cc_out = nc.dram_tensor("cc_out", [1, 128], F32)

    ctx = contextlib.ExitStack()
    sb = lambda name, shape, dt: ctx.enter_context(nc.sbuf_tensor(name, shape, dt))
    ps = lambda name, shape: ctx.enter_context(nc.psum_tensor(name, shape, F32))

    x1 = sb("x1", [128, NT * D], F32)
    wqkvT = sb("wqkvT", [128, 8 * 3 * D], BF16)
    woutT = sb("woutT", [128, 8 * D], BF16)
    wffnT = sb("wffnT", [128, 8 * D], BF16)
    qkvb_r = sb("qkvb_r", [1, 3 * D], BF16)
    outb_r = sb("outb_r", [1, D], BF16)
    bffn_r = sb("bffn_r", [1, D], F32)
    bffn_i = sb("bffn_i", [1, D], BF16)
    ones_r = sb("ones_r", [1, 512], BF16)
    ones_c = sb("ones_c", [128, 1], BF16)
    mask = sb("mask", [128, 128], BF16)
    identf = sb("identf", [128, 128], F32)
    iota_f = sb("iota_f", [128, 128], F32)
    iota_p = sb("iota_p", [128, 1], F32)
    S_sb = sb("S_sb", [128, 512], BF16)
    Z_sb = sb("Z_sb", [128, 8], BF16)
    scr = sb("scr", [128, D], F32)
    sum1 = sb("sum1", [128, NT], F32)
    sq1 = sb("sq1", [128, NT], F32)
    mean1 = sb("mean1", [128, NT], F32)
    rstd1 = sb("rstd1", [128, NT], F32)
    var1 = sb("var1", [128, NT], F32)
    sum2 = sb("sum2", [128, NT], F32)
    sq2 = sb("sq2", [128, NT], F32)
    mean2 = sb("mean2", [128, NT], F32)
    rstd2 = sb("rstd2", [128, NT], F32)
    var2 = sb("var2", [128, NT], F32)
    rstd2s = sb("rstd2s", [128, NT], F32)
    m2s = sb("m2s", [128, NT], F32)
    hmax = sb("hmax", [128, NT], F32)
    hmaxc = sb("hmaxc", [128, 1], F32)
    den16 = sb("den16", [128, 16], F32)
    rden16 = sb("rden16", [128, 16], F32)
    xsw = sb("xsw", [128, 16], F32)
    invw = sb("invw", [128, 16], F32)
    nsc = sb("nsc", [128, 32], F32)

    psD = [ps(f"psD{i}", [128, 512]) for i in range(2)]
    psN = [ps(f"psN{i}", [128, 512]) for i in range(2)]
    psA = [ps(f"psA{i}", [128, 512]) for i in range(2)]
    psP = [ps(f"psP{i}", [128, 512]) for i in range(2)]

    ld = Sem(ctx, nc, "ld")
    ld2 = Sem(ctx, nc, "ld2")
    ld3 = Sem(ctx, nc, "ld3")
    lda = Sem(ctx, nc, "lda")
    ldb = Sem(ctx, nc, "ldb")
    tph = Sem(ctx, nc, "tph")
    tpk = Sem(ctx, nc, "tpk")
    tpa = Sem(ctx, nc, "tpa")
    tpx = Sem(ctx, nc, "tpx")
    st = Sem(ctx, nc, "st")
    mm = Sem(ctx, nc, "mm")
    vv = Sem(ctx, nc, "vv")
    ss = Sem(ctx, nc, "ss")
    gp = Sem(ctx, nc, "gp")
    cs = Sem(ctx, nc, "cs")

    P = Prog()

    def dma(eng, out, in_, sem, **kw):
        tok = (sem, sem.bump(16))
        def fn(e, o=out, i=in_, k=kw, s=sem):
            e.dma_start(out=o, in_=i, **k).then_inc(s.h, 16)
        P.emit(eng, fn)
        return tok

    def op(eng, sem, fn):
        tok = (sem, sem.bump(1))
        def fn2(e, f=fn, s=sem):
            f(e).then_inc(s.h, 1)
        P.emit(eng, fn2)
        return tok

    def op_(eng, fn):
        P.emit(eng, fn)

    def vsync():
        P.wait("vector", (vv, vv.n))

    def ssync():
        P.wait("scalar", (ss, ss.n))

    def emit_ln_finalize(sum_t, sq_t, mean_t, var_t, rstd_t, wait_tok, dbg=None):
        P.wait("vector", wait_tok)
        vsync()
        t_mn = op("vector", vv, lambda e: e.tensor_scalar(
            mean_t[:, :], sum_t[:, :], 1.0 / D, None, ALU.mult))
        t_mv = op("vector", vv, lambda e: e.tensor_scalar(
            var_t[:, :], sq_t[:, :], 1.0 / D, None, ALU.mult))
        P.waits("scalar", [t_mn, t_mv])
        ssync()
        t_m2 = op("scalar", ss, lambda e: e.square(m2s[:, :], mean_t[:, :]))
        P.wait("vector", t_m2)
        vsync()
        t_v = op("vector", vv, lambda e: e.scalar_tensor_tensor(
            var_t[:, :], var_t[:, :], LNEPS, m2s[:, :],
            ALU.add, ALU.subtract))
        if dbg:
            tap(dbg + "_sum", sum_t[:, :], [128, NT], F32, [t_v])
            tap(dbg + "_sq", sq_t[:, :], [128, NT], F32, [t_v])
            tap(dbg + "_var", var_t[:, :], [128, NT], F32, [t_v])
        P.wait("scalar", t_v)
        ssync()
        t_l = op("scalar", ss, lambda e: e.activation(
            nsc[:, 0:16], var_t[:, :], ACT.Ln))
        P.wait("scalar", t_l)
        t_e = op("scalar", ss, lambda e: e.activation(
            rstd_t[:, :], nsc[:, 0:16], ACT.Exp, scale=-0.5))
        if dbg:
            tap(dbg + "_ln", nsc[:, 0:16], [128, NT], F32, [t_e])
        return t_e

    tap_names = []

    def tap(name, ap, shape, dt, wait_toks):
        if not debug_taps:
            return
        td = nc.dram_tensor("tap_" + name, shape, dt, kind="ExternalOutput")
        tap_names.append("tap_" + name)
        P.waits("gpsimd", wait_toks)
        return dma("gpsimd", td[:, :], ap, ld)

    t_iota = [None, None]
    t_iota[0] = op("gpsimd", gp, lambda e: e.iota(
        iota_f[:, :], [[1, 128]], channel_multiplier=0,
        allow_small_or_imprecise_dtypes=True))
    t_iota[1] = op("gpsimd", gp, lambda e: e.iota(
        iota_p[:, :], [[1, 1]], channel_multiplier=1,
        allow_small_or_imprecise_dtypes=True))
    op("gpsimd", gp, lambda e: e.memset(ones_r[:, :], 1.0))
    op("gpsimd", gp, lambda e: e.memset(ones_c[:, :], 1.0))
    P.waits("gpsimd", t_iota)
    t_mask = op("gpsimd", gp, lambda e: e.tensor_scalar(
        mask[:, :], iota_f[:, :], iota_p[:, :], None, ALU.is_ge))
    t_id = op("gpsimd", gp, lambda e: e.tensor_scalar(
        identf[:, :], iota_f[:, :], iota_p[:, :], None, ALU.is_equal))
    t_consts = (gp, gp.n)
    P.wait("tensor", t_consts)
    P.wait("vector", t_consts)
    P.wait("scalar", t_consts)
    P.wait("sync", t_consts)

    x_r = x_d.ap().rearrange("(a p) d -> p a d", p=128)
    x1_r = x1.ap().rearrange("p (a d) -> p a d", d=D)

    dma("gpsimd", qkvb_r[:, :], qkvb_d[:, :], ld2)
    dma("gpsimd", outb_r[:, :], outb_d[:, :], ld2)
    dma("gpsimd", bffn_r[:, :], bffn_d[:, :], ld2)
    t_qkvb = t_outb = t_bffn = (ld2, ld2.n)

    qkvw_r = qkvw_d.ap().rearrange("(a p) d -> p a d", p=128)
    outw_r = outw_d.ap().rearrange("(a p) d -> p a d", p=128)
    w8_r = w8_d.ap().rearrange("(a p) d -> p a d", p=128)
    ws_r = ws_d.ap().rearrange("(a p) u -> p a u", p=128)
    psP_free = [None, None]
    with contextlib.ExitStack() as stgctx:
        stg = [stgctx.enter_context(nc.sbuf_tensor(f"stg{i}", [128, 4 * D], F32))
               for i in range(2)]
        w8stg = stgctx.enter_context(nc.sbuf_tensor("w8stg", [128, 4 * D], I8))
        wsstg = stgctx.enter_context(nc.sbuf_tensor("wsstg", [128, 8], F32))
        stg_r = [t.ap().rearrange("p (a d) -> p a d", d=D) for t in stg]
        stg_free = [None, None]
        w8_free = [None]

        def stage_piece(bi, load_src, dst, p4, scale_col=None):
            P.waits("gpsimd", [stg_free[bi]])
            if scale_col is None:
                lt = dma("gpsimd", stg_r[bi][:, :, :], load_src,
                         [lda, ldb][bi])
                src_tok = lt
            else:
                P.waits("gpsimd", [w8_free[0]])
                lt = dma("gpsimd", w8stg.ap().rearrange("p (a d) -> p a d", d=D),
                         load_src, ld3)
                P.waits("vector", [lt, t_ldws, stg_free[bi]])
                src_tok = None
                for k in range(4):
                    src_tok = op("vector", vv, lambda e, bi=bi, k=k, g=4 * p4 + k:
                                 e.tensor_scalar(stg[bi][:, k * D:(k + 1) * D],
                                                 w8stg[:, k * D:(k + 1) * D],
                                                 wsstg[:, g:g + 1], None, ALU.mult))
                w8_free[0] = src_tok
            ev = None
            for grp in range(8):
                pb = grp % 2
                P.waits("tensor", [src_tok, psP_free[pb], t_id])
                for j in range(4):
                    blk = grp * 4 + j
                    if j < 3:
                        P.emit("tensor", lambda e, pb=pb, j=j, blk=blk, bi=bi:
                               e.transpose(psP[pb][:, j * 128:(j + 1) * 128],
                                           stg[bi][:, blk * 128:blk * 128 + 128],
                                           identf[:, :]))
                tok_t = op("tensor", mm, lambda e, pb=pb, blk=grp * 4 + 3, bi=bi:
                           e.transpose(psP[pb][:, 384:512],
                                       stg[bi][:, blk * 128:blk * 128 + 128],
                                       identf[:, :]))
                P.wait("vector", tok_t)
                ev = op("vector", vv, lambda e, pb=pb, c0=p4 * 4096 + grp * 512:
                        e.tensor_scalar(dst[:, c0:c0 + 512], psP[pb][:, 0:512],
                                        0.0, 0.0, ALU.add, ALU.add))
                psP_free[pb] = ev
            stg_free[bi] = ev
            return ev

        t_ldws = dma("gpsimd", wsstg.ap().rearrange("p (a u) -> p a u", u=1),
                     ws_r, ld3, allow_slow_non_contiguous=True)
        wq_piece = []
        for p in range(2):
            wq_piece.append(stage_piece(p % 2, qkvw_r[:, 4 * p:4 * p + 4, :],
                                        wqkvT, p))
        t_x = []
        for xi in range(4):
            t_x.append(dma("gpsimd", x1_r[:, 4 * xi:4 * xi + 4, :],
                           x_r[:, 4 * xi:4 * xi + 4, :], ld))
        for p in range(2, 6):
            wq_piece.append(stage_piece(p % 2, qkvw_r[:, 4 * p:4 * p + 4, :],
                                        wqkvT, p))
        t_wq = wq_piece[-1]
        t_wo = None
        for p in range(2):
            t_wo = stage_piece(p % 2, outw_r[:, 4 * p:4 * p + 4, :], woutT, p)
        t_wf = None
        for p in range(2):
            t_wf = stage_piece(p % 2, w8_r[:, 4 * p:4 * p + 4, :], wffnT, p,
                               scale_col=True)
        tp_after_weights = t_wf

    woutT3 = woutT.ap().rearrange("p (g fc t) -> p fc g t", g=8, fc=8)
    wffnT3 = wffnT.ap().rearrange("p (g dc t) -> p dc g t", g=8, dc=8)
    wqkvT3 = wqkvT.ap().rearrange("p (ft dc t) -> p dc ft t", ft=24, dc=8)

    op("vector", vv, lambda e: e.memzero(sum1[:, :]))
    op("vector", vv, lambda e: e.memzero(sq1[:, :]))
    op("vector", vv, lambda e: e.memzero(sum2[:, :]))
    op("vector", vv, lambda e: e.memzero(sq2[:, :]))
    P.waits("vector", t_x)
    vsync()
    t_st1all = None
    for tt in range(NT):
        xs_ap = x1[:, tt * D:(tt + 1) * D]
        vsync()
        op("vector", vv, lambda e, a=xs_ap, c=tt: e.scalar_tensor_tensor(
            scr[:, :], a, 1.0, a, ALU.mult, ALU.mult,
            accum_out=sq1[:, c:c + 1]))
        t_st1all = op("vector", vv, lambda e, a=xs_ap, c=tt: e.tensor_reduce(
            sum1[:, c:c + 1], a, AX, ALU.add))
    t_rstd1 = emit_ln_finalize(sum1, sq1, mean1, var1, rstd1, t_st1all,
                               dbg="L1" if debug_taps else None)
    tap("rstd1e", rstd1[:, :], [128, NT], F32, [t_rstd1])

    S_f32 = sb("S_f32", [128, 512], F32)
    Z_f32 = sb("Z_f32", [128, 8], F32)
    qT = sb("qT", [128, 8 * BT], BF16)
    kT = sb("kT", [128, 8 * BT], BF16)
    ktm = sb("ktm", [128, TPB * D], BF16)
    vtm = sb("vtm", [128, TPB * D], BF16)
    attn = sb("attn", [128, TPB * D], F32)
    attnT = sb("attnT", [128, 8 * BT], BF16)
    am = [sb(f"am{i}", [128, 128], BF16) for i in range(2)]
    h1ctx = contextlib.ExitStack()
    h1 = h1ctx.enter_context(nc.sbuf_tensor("h1", [128, TPB * D], BF16))
    h1T = h1ctx.enter_context(nc.sbuf_tensor("h1T", [128, 8 * BT], BF16))

    tap("wqkvT", wqkvT[:, :], [128, 24 * D], BF16, [tp_after_weights])
    tap("woutT", woutT[:, :], [128, 8 * D], BF16, [tp_after_weights])
    tap("wffnT", wffnT[:, :], [128, 8 * D], BF16, [tp_after_weights])
    tap("mask", mask[:, :], [128, 128], BF16, [t_mask])
    tap("qkvb", qkvb_r[:, :], [1, 3 * D], BF16, [t_qkvb])

    psP_free = [None, None]
    psA_free = [None, None]
    psN_free = [None, None]
    am_free = [None, None]
    s_copy_tok = [None] * 4
    ds_free = [None]
    r_tok = [None] * NB
    e_toks = {}
    v_tok = [None] * NB
    stats1_tok = [None] * NB
    dbg_gate = []
    h1_tok = [None] * NB

    nproj = [0]

    def proj_group(n_out, lhsT_slices, rhs_slices, bias_lhsT, bias_rhs,
                   wait_toks):
        bi = nproj[0] % 2
        nproj[0] += 1
        bank = psP[bi]
        P.waits("tensor", list(wait_toks) + [psP_free[bi]])
        for i, (lt, rt) in enumerate(zip(lhsT_slices, rhs_slices)):
            P.emit("tensor", lambda e, o=bank[:, 0:n_out], lt=lt, rt=rt,
                   s=(i == 0): e.matmul(o, lt, rt, start=s, stop=False))
        tok = op("tensor", mm, lambda e, o=bank[:, 0:n_out], lt=bias_lhsT,
                 rt=bias_rhs: e.matmul(o, lt, rt, start=False, stop=True))
        return bank, tok, bi

    t_sfz = op("vector", vv, lambda e: e.memzero(S_f32[:, :]))
    t_sfz = op("vector", vv, lambda e: e.memzero(Z_f32[:, :]))
    t_sfz = (vv, t_sfz[1])

    def emit_h1(b, extra=()):
        tt0 = b * TPB
        P.waits("vector", list(extra) + [t_rstd1, h1T_done[0]])
        for i, tt in enumerate(range(tt0, tt0 + TPB)):
            h1_tok[b] = op("vector", vv, lambda e, tt=tt, i=i: e.tensor_scalar(
                h1[:, i * D:(i + 1) * D], x1[:, tt * D:(tt + 1) * D],
                mean1[:, tt:tt + 1], rstd1[:, tt:tt + 1],
                ALU.subtract, ALU.mult))

    def emit_h1T(b, extra=()):
        P.waits("sync", list(extra) + [h1_tok[b]])
        t = None
        for i in range(TPB):
            for dc in range(8):
                t = dma("sync", h1T[:, dc * BT + i * 128:dc * BT + i * 128 + 128],
                        h1[:, i * D + dc * 128:i * D + dc * 128 + 128],
                        tph, transpose=True)
        h1T_done[0] = t
        return t

    h1T_done = [None]
    t_h1T_next = [None]
    qkv_done = [None]

    for b in range(NB):
        tt0 = b * TPB
        if b == 0:
            emit_h1(0, [tp_after_weights])
            t_h1T = emit_h1T(0)
        else:
            t_h1T = t_h1T_next[0]

        qk_evac = [None] * 16
        for ft in range(16):
            lhsTs = [wqkvT[:, ft * 1024 + dc * 128:ft * 1024 + dc * 128 + 128]
                     for dc in range(8)]
            rhss = [h1T[:, dc * BT:(dc + 1) * BT] for dc in range(8)]
            bank, tok, bi = proj_group(
                BT, lhsTs, rhss,
                qkvb_r[0:1, ft * 128:ft * 128 + 128], ones_r[0:1, 0:BT],
                [t_h1T, wq_piece[ft // 4], t_qkvb, t_mask])
            dst = qT if ft < 8 else kT
            f0 = (ft % 8) * BT
            P.wait("vector", tok)
            qk_evac[ft] = op("vector", vv, lambda e, bank=bank, dst=dst, f0=f0:
                             e.tensor_scalar(dst[:, f0:f0 + BT], bank[:, 0:BT],
                                             0.0, EPSF, ALU.max, ALU.add))
            psP_free[bi] = qk_evac[ft]
        for i in range(TPB):
            for nb_ in range(2):
                lhsTs = [h1T[:, dc * BT + i * 128:dc * BT + i * 128 + 128]
                         for dc in range(8)]
                rhss = [wqkvT3[:, dc, 16 + nb_ * 4:16 + nb_ * 4 + 4, :]
                        for dc in range(8)]
                bank, tok, bi = proj_group(
                    512, lhsTs, rhss,
                    ones_r[0:1, 0:128], qkvb_r[0:1, 2048 + nb_ * 512:2048 + nb_ * 512 + 512],
                    [t_h1T, wq_piece[4 + nb_]])
                P.wait("scalar", tok)
                v_tok[b] = op("scalar", ss, lambda e, bank=bank, i=i, nb_=nb_:
                              e.copy(vtm[:, i * D + nb_ * 512:i * D + nb_ * 512 + 512],
                                     bank[:, 0:512]))
                psP_free[bi] = v_tok[b]
        qkv_done[0] = tok

        if b + 1 < NB:
            emit_h1(b + 1)


        P.waits("sync", [qk_evac[ft] for ft in range(8, 16)])
        t_ktm = None
        for ft in range(8):
            for i in range(TPB):
                t_ktm = dma("sync", ktm[:, i * D + ft * 128:i * D + ft * 128 + 128],
                            kT[:, ft * BT + i * 128:ft * BT + i * 128 + 128],
                            tpk, transpose=True)

        if b == 0:
            tap("h1", h1[:, :], [128, TPB * D], BF16, [h1_tok[0]])
            tap("h1T", h1T[:, :], [128, 8 * BT], BF16, [t_h1T])
            tap("qT", qT[:, :], [128, 8 * BT], BF16, qk_evac[:8])
            tap("kT", kT[:, :], [128, 8 * BT], BF16, qk_evac[8:])
            tap("vtm", vtm[:, :], [128, TPB * D], BF16, [v_tok[0]])
            tap("ktm", ktm[:, :], [128, TPB * D], BF16, [t_ktm])


        if b + 1 < NB:
            t_h1T_next[0] = emit_h1T(b + 1, [qkv_done[0]])

        aT_done = [None, None]
        for i in range(TPB):
            c = b * TPB + i
            t_attnT_wait = []
            last_num_tok = None
            for hg in range(4):
                nbk = hg % 2
                i_toks = []
                m_toks = []
                num_tok = None
                for j in range(4):
                    h = hg * 4 + j
                    ftile = h // 2
                    p0 = (h % 2) * 64
                    abk = h % 2
                    q_sl = qT[p0:p0 + 64, ftile * BT + i * 128:ftile * BT + i * 128 + 128]
                    k_sl = kT[p0:p0 + 64, ftile * BT + i * 128:ftile * BT + i * 128 + 128]
                    P.waits("tensor", [am_free[abk], psA_free[abk],
                                       qk_evac[8 + ftile], qk_evac[ftile]])
                    a_tok = op("tensor", mm, lambda e, abk=abk, k=k_sl, q=q_sl:
                               e.matmul(psA[abk][:, 0:128], k, q,
                                        start=True, stop=True))
                    meng = "vector"
                    msem = vv
                    P.waits(meng, [a_tok, am_free[abk], t_mask])
                    m_tok = op(meng, msem, lambda e, abk=abk:
                               e.tensor_tensor(am[abk][:, :], psA[abk][:, 0:128],
                                               mask[:, :], ALU.mult))
                    m_toks.append(m_tok)
                    psA_free[abk] = m_tok
                    wait_l = [m_tok, psN_free[nbk], v_tok[b]]
                    if c > 0:
                        wait_l.append(s_copy_tok[hg])
                    P.waits("tensor", wait_l)
                    o_num = psN[nbk][:, j * 65:j * 65 + 64]
                    o_den = psN[nbk][:, j * 65 + 64:j * 65 + 65]
                    v_sl = vtm[:, i * D + h * 64:i * D + h * 64 + 64]
                    sc0 = (h // 2) * 64
                    slot = [("am", o_num, v_sl), ("am", o_den, ones_c[:, :])]
                    if c > 0:
                        slot.append(("q", o_num, S_sb[p0:p0 + 64, sc0:sc0 + 64]))
                        slot.append(("q", o_den, Z_sb[p0:p0 + 64, h // 2:h // 2 + 1]))
                    for si, (kind, o_ap, r_ap) in enumerate(slot):
                        is_first = (j == 0 and si == 0)
                        is_last = (j == 3 and si == len(slot) - 1)
                        if kind == "am":
                            fn = (lambda e, o=o_ap, r=r_ap, abk=abk, s=is_first,
                                  p=is_last: e.matmul(o, am[abk][:, :], r,
                                                      start=s, stop=p))
                        else:
                            fn = (lambda e, o=o_ap, r=r_ap, q=q_sl, s=is_first,
                                  p=is_last: e.matmul(o, q, r, start=s, stop=p))
                        if is_last or si == len(slot) - 1:
                            tok2 = op("tensor", mm, fn)
                            if is_last:
                                num_tok = tok2
                            am_free[abk] = tok2
                        else:
                            P.emit("tensor", fn)
                P.waits("vector", [num_tok, e_toks.get((c - 1, hg))])
                den_src = bass.AP(psN[nbk], 64, [[512, 128], [65, 4]])
                d_tok = op("vector", vv, lambda e, src=den_src, hg=hg: e.tensor_scalar(
                    den16[:, hg * 4:hg * 4 + 4], src, EPSF, None, ALU.max))
                P.wait("scalar", d_tok)
                ssync()
                t_ln = op("scalar", ss, lambda e, hg=hg: e.activation(
                    nsc[:, 16:20], den16[:, hg * 4:hg * 4 + 4], ACT.Ln))
                P.wait("scalar", t_ln)
                op("scalar", ss, lambda e, hg=hg: e.activation(
                    rden16[:, hg * 4:hg * 4 + 4], nsc[:, 16:20], ACT.Exp,
                    scale=-1.0))
                r_tok2 = (ss, ss.n)
                ev = None
                P.wait("vector", r_tok2)
                for j in range(4):
                    h = hg * 4 + j
                    ev = op("vector", vv, lambda e, nbk=nbk, j=j, h=h, i=i, hg=hg:
                            e.tensor_scalar(attn[:, i * D + h * 64:i * D + h * 64 + 64],
                                            psN[nbk][:, j * 65:j * 65 + 64],
                                            rden16[:, hg * 4 + j:hg * 4 + j + 1],
                                            None, ALU.mult))
                psN_free[nbk] = ev
                e_toks[(c, hg)] = ev
                t_attnT_wait.append(ev)

            P.waits("tensor", [v_tok[b], t_ktm, ds_free[0]])
            up_toks_all = []
            for hg in range(4):
                dbank = psD[hg % 2]
                cb = (hg // 2) * 256
                for j in range(4):
                    h = hg * 4 + j
                    p0 = (j % 2) * 64
                    ksl = ktm[:, i * D + h * 64:i * D + h * 64 + 64]
                    vsl = vtm[:, i * D + h * 64:i * D + h * 64 + 64]
                    P.emit("tensor", lambda e, p0=p0, j=j, k=ksl, v=vsl,
                           dbank=dbank, cb=cb:
                           e.matmul(dbank[p0:p0 + 64,
                                          cb + (j // 2) * 64:cb + (j // 2) * 64 + 64],
                                    k, v, start=True, stop=True))
                for pr in range(2):
                    ksl = ktm[:, i * D + (hg * 4 + 2 * pr) * 64:
                              i * D + (hg * 4 + 2 * pr) * 64 + 128]
                    up = op("tensor", mm, lambda e, pr=pr, k=ksl,
                            dbank=dbank, cb=cb: e.matmul(
                                dbank[:, cb + 128 + pr:cb + 129 + pr],
                                k, ones_c[:, :],
                                start=True, stop=True))
                up_toks_all.append(up)


            P.waits("vector", up_toks_all + [(mm, mm.n)]
                    + ([t_sfz] if c == 0 else []))
            if debug_taps and c in (0, 1):
                t_dd = op("vector", vv, lambda e: e.tensor_scalar(
                    scr[:, 0:130], psD[0][:, 0:130], 0.0, 0.0,
                    ALU.add, ALU.add))
                tap(f"dbank{c}", scr[:, 0:130], [128, 130], F32, [t_dd])
            for hg in range(4):
                dbank = psD[hg % 2]
                cb = (hg // 2) * 256
                op("vector", vv, lambda e, hg=hg, dbank=dbank, cb=cb:
                   e.tensor_tensor(
                       S_f32[:, hg * 128:hg * 128 + 128],
                       dbank[:, cb:cb + 128],
                       S_f32[:, hg * 128:hg * 128 + 128], ALU.add))
                op("vector", vv, lambda e, hg=hg, dbank=dbank, cb=cb:
                   e.tensor_tensor(
                       Z_f32[:, hg * 2:hg * 2 + 2],
                       dbank[:, cb + 128:cb + 130],
                       Z_f32[:, hg * 2:hg * 2 + 2], ALU.add))
            vsync()
            for hg in range(4):
                op("vector", vv, lambda e, hg=hg: e.tensor_scalar(
                    S_sb[:, hg * 128:hg * 128 + 128],
                    S_f32[:, hg * 128:hg * 128 + 128], 0.0, 0.0,
                    ALU.add, ALU.add))
                s_copy_tok[hg] = op("vector", vv, lambda e, hg=hg: e.tensor_scalar(
                    Z_sb[:, hg * 2:hg * 2 + 2], Z_f32[:, hg * 2:hg * 2 + 2],
                    0.0, 0.0, ALU.add, ALU.add))
            ds_free[0] = s_copy_tok[3]

            if debug_taps and c == 0:
                dbg_gate.append(tap("den16c0", den16[:, :], [128, 16], F32,
                                    list(t_attnT_wait)))
                dbg_gate.append(tap("rden16c0", rden16[:, :], [128, 16], F32,
                                    list(t_attnT_wait)))
                dbg_gate.append(tap("attnc0", attn[:, 0:D], [128, D], BF16,
                                    list(t_attnT_wait)))

            aT_r = attnT.ap().rearrange("p (fc t) -> p fc t", t=BT)
            aT_toks = []
            for grp in range(2):
                pb = nproj[0] % 2
                nproj[0] += 1
                P.waits("tensor", t_attnT_wait + [psP_free[pb], t_id])
                for j in range(4):
                    blk = grp * 4 + j
                    if j < 3:
                        P.emit("tensor", lambda e, pb=pb, j=j, blk=blk, i=i:
                               e.transpose(psP[pb][:, j * 128:(j + 1) * 128],
                                           attn[:, i * D + blk * 128:i * D + blk * 128 + 128],
                                           identf[:, :]))
                tok_t = op("tensor", mm, lambda e, pb=pb, blk=grp * 4 + 3, i=i:
                           e.transpose(psP[pb][:, 384:512],
                                       attn[:, i * D + blk * 128:i * D + blk * 128 + 128],
                                       identf[:, :]))
                P.wait("vector", tok_t)
                ev_t = op("vector", vv, lambda e, pb=pb, grp=grp, i=i, aT_r=aT_r:
                          e.tensor_scalar(
                              aT_r[:, grp * 4:grp * 4 + 4, i * 128:i * 128 + 128],
                              psP[pb][:, 0:512], 0.0, 0.0, ALU.add, ALU.add))
                psP_free[pb] = ev_t
                aT_toks.append(ev_t)
            aT_done[i] = aT_toks

        if b == 0:
            tap("attn", attn[:, :], [128, TPB * D], BF16,
                [e_toks[(cc, hg)] for cc in range(2) for hg in range(4)])
            tap("den16", den16[:, :], [128, 16], F32,
                [e_toks[(1, hg)] for hg in range(4)])
            tap("rden16", rden16[:, :], [128, 16], F32,
                [e_toks[(1, hg)] for hg in range(4)])
            tap("S_sb", S_sb[:, :], [128, 512], BF16, list(s_copy_tok))
            tap("Z_sb", Z_sb[:, :], [128, 8], BF16, list(s_copy_tok))
            tap("psS", S_f32[:, :], [128, 512], F32, list(s_copy_tok))
            tap("psZ", Z_f32[:, :], [128, 8], F32, list(s_copy_tok))

        for i in range(TPB):
            tt = tt0 + i
            for nb_ in range(2):
                lhsTs = [attnT[:, fc * BT + i * 128:fc * BT + i * 128 + 128]
                         for fc in range(8)]
                rhss = [woutT3[:, fc, nb_ * 4:nb_ * 4 + 4, :]
                        for fc in range(8)]
                bank, tok, bi = proj_group(
                    512, lhsTs, rhss,
                    ones_r[0:1, 0:128], outb_r[0:1, nb_ * 512:nb_ * 512 + 512],
                    aT_done[i] + [t_outb, t_wo])
                P.wait("vector", tok)
                r_tok[b] = op("vector", vv, lambda e, bank=bank, tt=tt, nb_=nb_:
                              e.tensor_tensor(x1[:, tt * D + nb_ * 512:tt * D + nb_ * 512 + 512],
                                              bank[:, 0:512],
                                              x1[:, tt * D + nb_ * 512:tt * D + nb_ * 512 + 512],
                                              ALU.add))
                psP_free[bi] = r_tok[b]

        if b == 0:
            tap("x1b0", x1[:, 0:TPB * D], [128, TPB * D], F32, [r_tok[0]])

        P.wait("vector", r_tok[b])
        for tt in range(tt0, tt0 + TPB):
            xs_ap = x1[:, tt * D:(tt + 1) * D]
            vsync()
            op("vector", vv, lambda e, a=xs_ap, c=tt: e.scalar_tensor_tensor(
                scr[:, :], a, 1.0, a, ALU.mult, ALU.mult,
                accum_out=sq2[:, c:c + 1]))
            t_st2 = op("vector", vv, lambda e, a=xs_ap, c=tt: e.tensor_reduce(
                sum2[:, c:c + 1], a, AX, ALU.add))

        for tt in range(tt0, tt0 + TPB):
            vsync()
            op("vector", vv, lambda e, tt=tt: e.tensor_scalar(
                scr[:, :], x1[:, tt * D:(tt + 1) * D],
                float(D), sum2[:, tt:tt + 1],
                ALU.mult, ALU.subtract))
            vsync()
            op("vector", vv, lambda e, tt=tt: e.tensor_reduce(
                hmax[:, tt:tt + 1], scr[:, :], AX, ALU.max,
                apply_absolute_value=True))

    t_rstd2 = emit_ln_finalize(sum2, sq2, mean2, var2, rstd2, t_st2)
    P.wait("vector", t_rstd2)
    op("vector", vv, lambda e: e.tensor_tensor(
        hmax[:, :], rstd2[:, :], hmax[:, :], ALU.mult))

    vsync()
    t_hm = op("vector", vv, lambda e: e.tensor_reduce(
        hmaxc[:, :], hmax[:, :], AX, ALU.max))
    P.wait("gpsimd", t_hm)
    t_cc1 = dma("gpsimd", bass.AP(cc_in, 0, [[1, 128], [1, 1]]), hmaxc[:, :], ld)
    P.wait("gpsimd", t_cc1)
    tok_cc = (cs, cs.bump(1))
    P.emit("gpsimd", lambda e: e.collective_compute(
        "AllReduce", ALU.max, replica_groups=[list(range(8))],
        ins=[cc_in.ap().opt()], outs=[cc_out.ap().opt()]).then_inc(cs.h, 1))
    P.wait("gpsimd", tok_cc)
    t_cc2 = dma("gpsimd", scr[:, 0:128],
                bass.AP(cc_out, 0, [[0, 128], [1, 128]]), ld)
    P.wait("vector", t_cc2)
    for j in range(16):
        op("vector", vv, lambda e, j=j: e.tensor_reduce(
            nsc[:, j:j + 1], scr[:, 0:128], AX, ALU.max))
    vsync()
    t_xs0 = op("vector", vv, lambda e: e.tensor_scalar(
        xsw[:, :], nsc[:, 0:16], 1.0 / (127.0 * D), 1e-5, ALU.mult, ALU.max))
    P.waits("scalar", [t_xs0, t_rstd2])
    ssync()
    t_xl = op("scalar", ss, lambda e: e.activation(
        nsc[:, 0:16], xsw[:, :], ACT.Ln))
    P.wait("scalar", t_xl)
    t_xs = op("scalar", ss, lambda e: e.activation(
        invw[:, :], nsc[:, 0:16], ACT.Exp, scale=-1.0))
    P.waits("vector", [t_xs, t_bffn])
    t_bf = op("vector", vv, lambda e: e.tensor_scalar(
        bffn_i[:, :], bffn_r[:, :], invw[0:1, 0:1], None, ALU.mult))
    t_r2s = op("vector", vv, lambda e: e.tensor_scalar(
        rstd2s[:, :], rstd2[:, :], invw[:, 0:1], None, ALU.mult))

    tap("mean1", mean1[:, :], [128, NT], F32, [t_r2s])
    tap("rstd1", rstd1[:, :], [128, NT], F32, [t_r2s])
    tap("mean2", mean2[:, :], [128, NT], F32, [t_r2s])
    tap("rstd2", rstd2[:, :], [128, NT], F32, [t_r2s])
    tap("hmax", hmax[:, :], [128, NT], F32, [t_r2s])
    tap("xsw", xsw[:, :], [128, 16], F32, [t_r2s])
    tap("invw", invw[:, :], [128, 16], F32, [t_r2s])
    tap("bffn_i", bffn_i[:, :], [1, D], BF16, [t_bf])

    h1ctx.close()
    x8T = [sb(f"x8Tp{i}", [128, D], BF16) for i in range(2)]
    st_tok = None
    scr_free = None
    x8T_free = [None, None]
    o_tok = None
    for tt in range(NT):
        pbuf = tt % 2
        P.waits("vector", [t_r2s, scr_free])
        vsync()
        t_scr = op("vector", vv, lambda e, tt=tt: e.tensor_scalar(
            scr[:, :], x1[:, tt * D:(tt + 1) * D],
            mean2[:, tt:tt + 1], rstd2s[:, tt:tt + 1],
            ALU.subtract, ALU.mult))
        for grp in range(2):
            pb = nproj[0] % 2
            nproj[0] += 1
            P.waits("tensor", [t_scr, psP_free[pb], t_id, x8T_free[pbuf]])
            for j in range(4):
                blk = grp * 4 + j
                if j < 3:
                    P.emit("tensor", lambda e, pb=pb, j=j, blk=blk:
                           e.transpose(psP[pb][:, j * 128:(j + 1) * 128],
                                       scr[:, blk * 128:blk * 128 + 128],
                                       identf[:, :]))
            scr_free = op("tensor", mm, lambda e, pb=pb, blk=grp * 4 + 3:
                          e.transpose(psP[pb][:, 384:512],
                                      scr[:, blk * 128:blk * 128 + 128],
                                      identf[:, :]))
            P.wait("vector", scr_free)
            ev = op("vector", vv, lambda e, pb=pb, pbuf=pbuf, c0=grp * 512:
                    e.tensor_scalar(x8T[pbuf][:, c0:c0 + 512],
                                    psP[pb][:, 0:512],
                                    MAGIC, MAGIC, ALU.add, ALU.subtract))
            psP_free[pb] = ev
            x8T_ready = ev
        for nb_ in range(2):
            lhsTs = [x8T[pbuf][:, dc * 128:dc * 128 + 128]
                     for dc in range(8)]
            rhss = [wffnT3[:, dc, nb_ * 4:nb_ * 4 + 4, :]
                    for dc in range(8)]
            bank, tok, bi = proj_group(
                512, lhsTs, rhss,
                ones_r[0:1, 0:128], bffn_i[0:1, nb_ * 512:nb_ * 512 + 512],
                [x8T_ready, t_wf, t_bf])
            x8T_free[pbuf] = tok
            P.waits("vector", [tok, t_xs])
            o_tok = op("vector", vv, lambda e, bank=bank, tt=tt, nb_=nb_:
                       e.scalar_tensor_tensor(
                           x1[:, tt * D + nb_ * 512:tt * D + nb_ * 512 + 512],
                           bank[:, 0:512], xsw[:, 0:1],
                           x1[:, tt * D + nb_ * 512:tt * D + nb_ * 512 + 512],
                           ALU.mult, ALU.add))
            psP_free[bi] = o_tok
        if tt % 2 == 1:
            P.wait("gpsimd", o_tok)
            out_r = out_d.ap().rearrange("(a p) d -> p a d", p=128)
            st_tok = dma("gpsimd", out_r[:, tt - 1:tt + 1, :],
                         x1_r[:, tt - 1:tt + 1, :], st)

    P.wait("gpsimd", (st, st.n))

    with ctx, nc.Block() as block:
        @block.gpsimd
        def _(e):
            for f in P.ops["gpsimd"]:
                f(e)

        @block.sync
        def _(e):
            for f in P.ops["sync"]:
                f(e)

        @block.tensor
        def _(e):
            for f in P.ops["tensor"]:
                f(e)

        @block.vector
        def _(e):
            for f in P.ops["vector"]:
                f(e)

        @block.scalar
        def _(e):
            for f in P.ops["scalar"]:
                f(e)

    return nc


_NC_CACHE = None


def kernel(L0_attn_x, L0_attn_qkv_w, L0_attn_qkv_b, L0_attn_out_w,
           L0_attn_out_b, L0_ffn_weight_int8, L0_ffn_weight_scale,
           L0_ffn_bias):
    global _NC_CACHE
    if _NC_CACHE is None:
        _NC_CACHE = build_nc()
    nc = _NC_CACHE
    in_maps = []
    for b in range(8):
        in_maps.append({
            "x": np.ascontiguousarray(L0_attn_x[b]),
            "qkvw": np.ascontiguousarray(L0_attn_qkv_w),
            "qkvb": np.ascontiguousarray(L0_attn_qkv_b.reshape(1, -1)),
            "outw": np.ascontiguousarray(L0_attn_out_w),
            "outb": np.ascontiguousarray(L0_attn_out_b.reshape(1, -1)),
            "w8": np.ascontiguousarray(L0_ffn_weight_int8),
            "ws": np.ascontiguousarray(L0_ffn_weight_scale),
            "bffn": np.ascontiguousarray(L0_ffn_bias.reshape(1, -1)),
        })
    res = run_bass_kernel_spmd(nc, in_maps, core_ids=list(range(8)))
    return np.stack([r["out"] for r in res.results]).astype(np.float32)


# version 46
# speedup vs baseline: 1.0356x; 1.0356x over previous
import contextlib
import numpy as np

import concourse.bass as bass
import concourse.mybir as mybir
from concourse.bass_utils import run_bass_kernel_spmd

F32 = mybir.dt.float32
BF16 = mybir.dt.bfloat16
I8 = mybir.dt.int8
ALU = mybir.AluOpType
ACT = mybir.ActivationFunctionType
AX = mybir.AxisListType.X

L = 2048
D = 1024
H = 16
DH = 64
C = 128
NT = L // C
TPB = 2
NB = NT // TPB
BT = TPB * C
MAGIC = float(np.float32(12582912.0))
EPSF = 1e-6
LNEPS = 1e-5


class Sem:
    def __init__(self, ctx, nc, name):
        self.h = ctx.enter_context(nc.semaphore(name))
        self.n = 0

    def bump(self, k=1):
        self.n += k
        return self.n


class Prog:
    ENGINES = ("gpsimd", "sync", "tensor", "vector", "scalar")

    def __init__(self):
        self.ops = {e: [] for e in self.ENGINES}

    def emit(self, eng, fn):
        self.ops[eng].append(fn)

    def wait(self, eng, tok):
        if tok is None:
            return
        sem, val = tok
        self.ops[eng].append(lambda e, s=sem, v=val: e.wait_ge(s.h, v))

    def waits(self, eng, toks):
        best = {}
        for t in toks:
            if t is None:
                continue
            s, v = t
            if s not in best or v > best[s]:
                best[s] = v
        for s, v in best.items():
            self.ops[eng].append(lambda e, s=s, v=v: e.wait_ge(s.h, v))


def build_nc(debug_taps=False):
    nc = bass.Bass("TRN2", target_bir_lowering=False)

    x_d = nc.dram_tensor("x", [L, D], F32, kind="ExternalInput")
    qkvw_d = nc.dram_tensor("qkvw", [3 * D, D], F32, kind="ExternalInput")
    qkvb_d = nc.dram_tensor("qkvb", [1, 3 * D], F32, kind="ExternalInput")
    outw_d = nc.dram_tensor("outw", [D, D], F32, kind="ExternalInput")
    outb_d = nc.dram_tensor("outb", [1, D], F32, kind="ExternalInput")
    w8_d = nc.dram_tensor("w8", [D, D], I8, kind="ExternalInput")
    ws_d = nc.dram_tensor("ws", [D, 1], F32, kind="ExternalInput")
    bffn_d = nc.dram_tensor("bffn", [1, D], F32, kind="ExternalInput")
    out_d = nc.dram_tensor("out", [L, D], F32, kind="ExternalOutput")

    cc_in = nc.dram_tensor("cc_in", [1, 128], F32)
    cc_out = nc.dram_tensor("cc_out", [1, 128], F32)

    ctx = contextlib.ExitStack()
    sb = lambda name, shape, dt: ctx.enter_context(nc.sbuf_tensor(name, shape, dt))
    ps = lambda name, shape: ctx.enter_context(nc.psum_tensor(name, shape, F32))

    x1 = sb("x1", [128, NT * D], F32)
    wqkvT = sb("wqkvT", [128, 8 * 3 * D], BF16)
    woutT = sb("woutT", [128, 8 * D], BF16)
    wffnT = sb("wffnT", [128, 8 * D], BF16)
    qkvb_r = sb("qkvb_r", [1, 3 * D], BF16)
    outb_r = sb("outb_r", [1, D], BF16)
    bffn_r = sb("bffn_r", [1, D], F32)
    bffn_i = sb("bffn_i", [1, D], BF16)
    ones_r = sb("ones_r", [1, 512], BF16)
    ones_c = sb("ones_c", [128, 1], BF16)
    mask = sb("mask", [128, 128], BF16)
    identf = sb("identf", [128, 128], F32)
    iota_f = sb("iota_f", [128, 128], F32)
    iota_p = sb("iota_p", [128, 1], F32)
    S_sb = sb("S_sb", [128, 512], BF16)
    Z_sb = sb("Z_sb", [128, 8], BF16)
    scr = sb("scr", [128, D], F32)
    sum1 = sb("sum1", [128, NT], F32)
    sq1 = sb("sq1", [128, NT], F32)
    mean1 = sb("mean1", [128, NT], F32)
    rstd1 = sb("rstd1", [128, NT], F32)
    var1 = sb("var1", [128, NT], F32)
    sum2 = sb("sum2", [128, NT], F32)
    sq2 = sb("sq2", [128, NT], F32)
    mean2 = sb("mean2", [128, NT], F32)
    rstd2 = sb("rstd2", [128, NT], F32)
    var2 = sb("var2", [128, NT], F32)
    rstd2s = sb("rstd2s", [128, NT], F32)
    m2s = sb("m2s", [128, NT], F32)
    hmax = sb("hmax", [128, NT], F32)
    hmaxc = sb("hmaxc", [128, 1], F32)
    den16 = sb("den16", [128, 16], F32)
    rden16 = sb("rden16", [128, 16], F32)
    xsw = sb("xsw", [128, 16], F32)
    invw = sb("invw", [128, 16], F32)
    nsc = sb("nsc", [128, 32], F32)

    psD = [ps(f"psD{i}", [128, 512]) for i in range(2)]
    psN = [ps(f"psN{i}", [128, 512]) for i in range(2)]
    psA = [ps(f"psA{i}", [128, 512]) for i in range(2)]
    psP = [ps(f"psP{i}", [128, 512]) for i in range(2)]

    ld = Sem(ctx, nc, "ld")
    ld2 = Sem(ctx, nc, "ld2")
    ld3 = Sem(ctx, nc, "ld3")
    lda = Sem(ctx, nc, "lda")
    ldb = Sem(ctx, nc, "ldb")
    tph = Sem(ctx, nc, "tph")
    tpk = Sem(ctx, nc, "tpk")
    tpa = Sem(ctx, nc, "tpa")
    tpx = Sem(ctx, nc, "tpx")
    st = Sem(ctx, nc, "st")
    mm = Sem(ctx, nc, "mm")
    vv = Sem(ctx, nc, "vv")
    ss = Sem(ctx, nc, "ss")
    gp = Sem(ctx, nc, "gp")
    cs = Sem(ctx, nc, "cs")

    P = Prog()

    def dma(eng, out, in_, sem, **kw):
        tok = (sem, sem.bump(16))
        def fn(e, o=out, i=in_, k=kw, s=sem):
            e.dma_start(out=o, in_=i, **k).then_inc(s.h, 16)
        P.emit(eng, fn)
        return tok

    def op(eng, sem, fn):
        tok = (sem, sem.bump(1))
        def fn2(e, f=fn, s=sem):
            f(e).then_inc(s.h, 1)
        P.emit(eng, fn2)
        return tok

    def op_(eng, fn):
        P.emit(eng, fn)

    def vsync():
        P.wait("vector", (vv, vv.n))

    def ssync():
        P.wait("scalar", (ss, ss.n))

    def emit_ln_finalize(sum_t, sq_t, mean_t, var_t, rstd_t, wait_tok, dbg=None):
        P.wait("vector", wait_tok)
        vsync()
        t_mn = op("vector", vv, lambda e: e.tensor_scalar(
            mean_t[:, :], sum_t[:, :], 1.0 / D, None, ALU.mult))
        t_mv = op("vector", vv, lambda e: e.tensor_scalar(
            var_t[:, :], sq_t[:, :], 1.0 / D, None, ALU.mult))
        P.waits("scalar", [t_mn, t_mv])
        ssync()
        t_m2 = op("scalar", ss, lambda e: e.square(m2s[:, :], mean_t[:, :]))
        P.wait("vector", t_m2)
        vsync()
        t_v = op("vector", vv, lambda e: e.scalar_tensor_tensor(
            var_t[:, :], var_t[:, :], LNEPS, m2s[:, :],
            ALU.add, ALU.subtract))
        if dbg:
            tap(dbg + "_sum", sum_t[:, :], [128, NT], F32, [t_v])
            tap(dbg + "_sq", sq_t[:, :], [128, NT], F32, [t_v])
            tap(dbg + "_var", var_t[:, :], [128, NT], F32, [t_v])
        P.wait("scalar", t_v)
        ssync()
        t_l = op("scalar", ss, lambda e: e.activation(
            nsc[:, 0:16], var_t[:, :], ACT.Ln))
        P.wait("scalar", t_l)
        t_e = op("scalar", ss, lambda e: e.activation(
            rstd_t[:, :], nsc[:, 0:16], ACT.Exp, scale=-0.5))
        if dbg:
            tap(dbg + "_ln", nsc[:, 0:16], [128, NT], F32, [t_e])
        return t_e

    tap_names = []

    def tap(name, ap, shape, dt, wait_toks):
        if not debug_taps:
            return
        td = nc.dram_tensor("tap_" + name, shape, dt, kind="ExternalOutput")
        tap_names.append("tap_" + name)
        P.waits("gpsimd", wait_toks)
        return dma("gpsimd", td[:, :], ap, ld)

    t_iota = [None, None]
    t_iota[0] = op("gpsimd", gp, lambda e: e.iota(
        iota_f[:, :], [[1, 128]], channel_multiplier=0,
        allow_small_or_imprecise_dtypes=True))
    t_iota[1] = op("gpsimd", gp, lambda e: e.iota(
        iota_p[:, :], [[1, 1]], channel_multiplier=1,
        allow_small_or_imprecise_dtypes=True))
    op("gpsimd", gp, lambda e: e.memset(ones_r[:, :], 1.0))
    op("gpsimd", gp, lambda e: e.memset(ones_c[:, :], 1.0))
    P.waits("gpsimd", t_iota)
    t_mask = op("gpsimd", gp, lambda e: e.tensor_scalar(
        mask[:, :], iota_f[:, :], iota_p[:, :], None, ALU.is_ge))
    t_id = op("gpsimd", gp, lambda e: e.tensor_scalar(
        identf[:, :], iota_f[:, :], iota_p[:, :], None, ALU.is_equal))
    t_consts = (gp, gp.n)
    P.wait("tensor", t_consts)
    P.wait("vector", t_consts)
    P.wait("scalar", t_consts)
    P.wait("sync", t_consts)

    x_r = x_d.ap().rearrange("(a p) d -> p a d", p=128)
    x1_r = x1.ap().rearrange("p (a d) -> p a d", d=D)

    dma("gpsimd", qkvb_r[:, :], qkvb_d[:, :], ld2)
    dma("gpsimd", outb_r[:, :], outb_d[:, :], ld2)
    dma("gpsimd", bffn_r[:, :], bffn_d[:, :], ld2)
    t_qkvb = t_outb = t_bffn = (ld2, ld2.n)

    qkvw_r = qkvw_d.ap().rearrange("(a p) d -> p a d", p=128)
    outw_r = outw_d.ap().rearrange("(a p) d -> p a d", p=128)
    w8_r = w8_d.ap().rearrange("(a p) d -> p a d", p=128)
    ws_r = ws_d.ap().rearrange("(a p) u -> p a u", p=128)
    psP_free = [None, None]
    with contextlib.ExitStack() as stgctx:
        stg = [stgctx.enter_context(nc.sbuf_tensor(f"stg{i}", [128, 4 * D], F32))
               for i in range(2)]
        w8stg = stgctx.enter_context(nc.sbuf_tensor("w8stg", [128, 4 * D], I8))
        wsstg = stgctx.enter_context(nc.sbuf_tensor("wsstg", [128, 8], F32))
        stg_r = [t.ap().rearrange("p (a d) -> p a d", d=D) for t in stg]
        stg_free = [None, None]
        w8_free = [None]

        def stage_piece(bi, load_src, dst, p4, scale_col=None):
            P.waits("gpsimd", [stg_free[bi]])
            if scale_col is None:
                lt = dma("gpsimd", stg_r[bi][:, :, :], load_src,
                         [lda, ldb][bi])
                src_tok = lt
            else:
                P.waits("gpsimd", [w8_free[0]])
                lt = dma("gpsimd", w8stg.ap().rearrange("p (a d) -> p a d", d=D),
                         load_src, ld3)
                P.waits("vector", [lt, t_ldws, stg_free[bi]])
                src_tok = None
                for k in range(4):
                    src_tok = op("vector", vv, lambda e, bi=bi, k=k, g=4 * p4 + k:
                                 e.tensor_scalar(stg[bi][:, k * D:(k + 1) * D],
                                                 w8stg[:, k * D:(k + 1) * D],
                                                 wsstg[:, g:g + 1], None, ALU.mult))
                w8_free[0] = src_tok
            ev = None
            for grp in range(8):
                pb = grp % 2
                P.waits("tensor", [src_tok, psP_free[pb], t_id])
                for j in range(4):
                    blk = grp * 4 + j
                    if j < 3:
                        P.emit("tensor", lambda e, pb=pb, j=j, blk=blk, bi=bi:
                               e.transpose(psP[pb][:, j * 128:(j + 1) * 128],
                                           stg[bi][:, blk * 128:blk * 128 + 128],
                                           identf[:, :]))
                tok_t = op("tensor", mm, lambda e, pb=pb, blk=grp * 4 + 3, bi=bi:
                           e.transpose(psP[pb][:, 384:512],
                                       stg[bi][:, blk * 128:blk * 128 + 128],
                                       identf[:, :]))
                P.wait("vector", tok_t)
                ev = op("vector", vv, lambda e, pb=pb, c0=p4 * 4096 + grp * 512:
                        e.tensor_scalar(dst[:, c0:c0 + 512], psP[pb][:, 0:512],
                                        0.0, 0.0, ALU.add, ALU.add))
                psP_free[pb] = ev
            stg_free[bi] = ev
            return ev

        t_ldws = dma("gpsimd", wsstg.ap().rearrange("p (a u) -> p a u", u=1),
                     ws_r, ld3, allow_slow_non_contiguous=True)
        wq_piece = []
        for p in range(2):
            wq_piece.append(stage_piece(p % 2, qkvw_r[:, 4 * p:4 * p + 4, :],
                                        wqkvT, p))
        t_x = []
        for xi in range(4):
            t_x.append(dma("gpsimd", x1_r[:, 4 * xi:4 * xi + 4, :],
                           x_r[:, 4 * xi:4 * xi + 4, :], ld))
        for p in range(2, 6):
            wq_piece.append(stage_piece(p % 2, qkvw_r[:, 4 * p:4 * p + 4, :],
                                        wqkvT, p))
        t_wq = wq_piece[-1]
        t_wo = None
        for p in range(2):
            t_wo = stage_piece(p % 2, outw_r[:, 4 * p:4 * p + 4, :], woutT, p)
        t_wf = None
        for p in range(2):
            t_wf = stage_piece(p % 2, w8_r[:, 4 * p:4 * p + 4, :], wffnT, p,
                               scale_col=True)
        tp_after_weights = t_wf

    woutT3 = woutT.ap().rearrange("p (g fc t) -> p fc g t", g=8, fc=8)
    wffnT3 = wffnT.ap().rearrange("p (g dc t) -> p dc g t", g=8, dc=8)
    wqkvT3 = wqkvT.ap().rearrange("p (ft dc t) -> p dc ft t", ft=24, dc=8)

    op("vector", vv, lambda e: e.memzero(sum1[:, :]))
    op("vector", vv, lambda e: e.memzero(sq1[:, :]))
    op("vector", vv, lambda e: e.memzero(sum2[:, :]))
    op("vector", vv, lambda e: e.memzero(sq2[:, :]))
    P.waits("vector", t_x)
    vsync()
    t_st1all = None
    for tt in range(NT):
        xs_ap = x1[:, tt * D:(tt + 1) * D]
        vsync()
        op("vector", vv, lambda e, a=xs_ap, c=tt: e.scalar_tensor_tensor(
            scr[:, :], a, 1.0, a, ALU.mult, ALU.mult,
            accum_out=sq1[:, c:c + 1]))
        t_st1all = op("vector", vv, lambda e, a=xs_ap, c=tt: e.tensor_reduce(
            sum1[:, c:c + 1], a, AX, ALU.add))
    t_rstd1 = emit_ln_finalize(sum1, sq1, mean1, var1, rstd1, t_st1all,
                               dbg="L1" if debug_taps else None)
    tap("rstd1e", rstd1[:, :], [128, NT], F32, [t_rstd1])

    S_f32 = sb("S_f32", [128, 512], F32)
    Z_f32 = sb("Z_f32", [128, 8], F32)
    qT = sb("qT", [128, 8 * BT], BF16)
    kT = sb("kT", [128, 8 * BT], BF16)
    ktm = sb("ktm", [128, TPB * D], BF16)
    vtm = sb("vtm", [128, TPB * D], BF16)
    attn = sb("attn", [128, TPB * D], F32)
    attnT = sb("attnT", [128, 8 * BT], BF16)
    am = [sb(f"am{i}", [128, 128], BF16) for i in range(2)]
    h1ctx = contextlib.ExitStack()
    h1 = h1ctx.enter_context(nc.sbuf_tensor("h1", [128, TPB * D], BF16))
    h1T = h1ctx.enter_context(nc.sbuf_tensor("h1T", [128, 8 * BT], BF16))

    tap("wqkvT", wqkvT[:, :], [128, 24 * D], BF16, [tp_after_weights])
    tap("woutT", woutT[:, :], [128, 8 * D], BF16, [tp_after_weights])
    tap("wffnT", wffnT[:, :], [128, 8 * D], BF16, [tp_after_weights])
    tap("mask", mask[:, :], [128, 128], BF16, [t_mask])
    tap("qkvb", qkvb_r[:, :], [1, 3 * D], BF16, [t_qkvb])

    psP_free = [None, None]
    psA_free = [None, None]
    psN_free = [None, None]
    am_free = [None, None]
    s_copy_tok = [None] * 4
    ds_free = [None]
    r_tok = [None] * NB
    e_toks = {}
    v_tok = [None] * NB
    stats1_tok = [None] * NB
    dbg_gate = []
    h1_tok = [None] * NB

    nproj = [0]

    def proj_group(n_out, lhsT_slices, rhs_slices, bias_lhsT, bias_rhs,
                   wait_toks):
        bi = nproj[0] % 2
        nproj[0] += 1
        bank = psP[bi]
        P.waits("tensor", list(wait_toks) + [psP_free[bi]])
        for i, (lt, rt) in enumerate(zip(lhsT_slices, rhs_slices)):
            P.emit("tensor", lambda e, o=bank[:, 0:n_out], lt=lt, rt=rt,
                   s=(i == 0): e.matmul(o, lt, rt, start=s, stop=False))
        tok = op("tensor", mm, lambda e, o=bank[:, 0:n_out], lt=bias_lhsT,
                 rt=bias_rhs: e.matmul(o, lt, rt, start=False, stop=True))
        return bank, tok, bi

    t_sfz = op("vector", vv, lambda e: e.memzero(S_f32[:, :]))
    t_sfz = op("vector", vv, lambda e: e.memzero(Z_f32[:, :]))
    t_sfz = (vv, t_sfz[1])

    def emit_h1(b, extra=()):
        tt0 = b * TPB
        P.waits("vector", list(extra) + [t_rstd1, h1T_done[0]])
        for i, tt in enumerate(range(tt0, tt0 + TPB)):
            h1_tok[b] = op("vector", vv, lambda e, tt=tt, i=i: e.tensor_scalar(
                h1[:, i * D:(i + 1) * D], x1[:, tt * D:(tt + 1) * D],
                mean1[:, tt:tt + 1], rstd1[:, tt:tt + 1],
                ALU.subtract, ALU.mult))

    def emit_h1T(b, extra=()):
        P.waits("sync", list(extra) + [h1_tok[b]])
        t = None
        for i in range(TPB):
            for dc in range(8):
                t = dma("sync", h1T[:, dc * BT + i * 128:dc * BT + i * 128 + 128],
                        h1[:, i * D + dc * 128:i * D + dc * 128 + 128],
                        tph, transpose=True)
        h1T_done[0] = t
        return t

    h1T_done = [None]
    t_h1T_next = [None]
    qkv_done = [None]

    for b in range(NB):
        tt0 = b * TPB
        if b == 0:
            emit_h1(0, [tp_after_weights])
            t_h1T = emit_h1T(0)
        else:
            t_h1T = t_h1T_next[0]

        qk_evac = [None] * 16
        for ft in range(16):
            lhsTs = [wqkvT[:, ft * 1024 + dc * 128:ft * 1024 + dc * 128 + 128]
                     for dc in range(8)]
            rhss = [h1T[:, dc * BT:(dc + 1) * BT] for dc in range(8)]
            bank, tok, bi = proj_group(
                BT, lhsTs, rhss,
                qkvb_r[0:1, ft * 128:ft * 128 + 128], ones_r[0:1, 0:BT],
                [t_h1T, wq_piece[ft // 4], t_qkvb, t_mask])
            dst = qT if ft < 8 else kT
            f0 = (ft % 8) * BT
            P.wait("vector", tok)
            qk_evac[ft] = op("vector", vv, lambda e, bank=bank, dst=dst, f0=f0:
                             e.tensor_scalar(dst[:, f0:f0 + BT], bank[:, 0:BT],
                                             0.0, EPSF, ALU.max, ALU.add))
            psP_free[bi] = qk_evac[ft]
        for i in range(TPB):
            for nb_ in range(2):
                lhsTs = [h1T[:, dc * BT + i * 128:dc * BT + i * 128 + 128]
                         for dc in range(8)]
                rhss = [wqkvT3[:, dc, 16 + nb_ * 4:16 + nb_ * 4 + 4, :]
                        for dc in range(8)]
                bank, tok, bi = proj_group(
                    512, lhsTs, rhss,
                    ones_r[0:1, 0:128], qkvb_r[0:1, 2048 + nb_ * 512:2048 + nb_ * 512 + 512],
                    [t_h1T, wq_piece[4 + nb_]])
                P.wait("scalar", tok)
                v_tok[b] = op("scalar", ss, lambda e, bank=bank, i=i, nb_=nb_:
                              e.copy(vtm[:, i * D + nb_ * 512:i * D + nb_ * 512 + 512],
                                     bank[:, 0:512]))
                psP_free[bi] = v_tok[b]
        qkv_done[0] = tok

        if b + 1 < NB:
            emit_h1(b + 1)


        P.waits("sync", [qk_evac[ft] for ft in range(8, 16)])
        t_ktm = None
        for ft in range(8):
            for i in range(TPB):
                t_ktm = dma("sync", ktm[:, i * D + ft * 128:i * D + ft * 128 + 128],
                            kT[:, ft * BT + i * 128:ft * BT + i * 128 + 128],
                            tpk, transpose=True)

        if b == 0:
            tap("h1", h1[:, :], [128, TPB * D], BF16, [h1_tok[0]])
            tap("h1T", h1T[:, :], [128, 8 * BT], BF16, [t_h1T])
            tap("qT", qT[:, :], [128, 8 * BT], BF16, qk_evac[:8])
            tap("kT", kT[:, :], [128, 8 * BT], BF16, qk_evac[8:])
            tap("vtm", vtm[:, :], [128, TPB * D], BF16, [v_tok[0]])
            tap("ktm", ktm[:, :], [128, TPB * D], BF16, [t_ktm])


        if b + 1 < NB:
            t_h1T_next[0] = emit_h1T(b + 1, [qkv_done[0]])

        aT_done = [None, None]
        for i in range(TPB):
            c = b * TPB + i
            t_attnT_wait = []
            last_num_tok = None
            for hg in range(4):
                nbk = hg % 2
                i_toks = []
                m_toks = []
                num_tok = None
                for j in range(4):
                    h = hg * 4 + j
                    ftile = h // 2
                    p0 = (h % 2) * 64
                    abk = h % 2
                    q_sl = qT[p0:p0 + 64, ftile * BT + i * 128:ftile * BT + i * 128 + 128]
                    k_sl = kT[p0:p0 + 64, ftile * BT + i * 128:ftile * BT + i * 128 + 128]
                    P.waits("tensor", [am_free[abk], psA_free[abk],
                                       qk_evac[8 + ftile], qk_evac[ftile]])
                    a_tok = op("tensor", mm, lambda e, abk=abk, k=k_sl, q=q_sl:
                               e.matmul(psA[abk][:, 0:128], k, q,
                                        start=True, stop=True))
                    meng = "vector"
                    msem = vv
                    P.waits(meng, [a_tok, am_free[abk], t_mask])
                    m_tok = op(meng, msem, lambda e, abk=abk:
                               e.tensor_tensor(am[abk][:, :], psA[abk][:, 0:128],
                                               mask[:, :], ALU.mult))
                    m_toks.append(m_tok)
                    psA_free[abk] = m_tok
                    wait_l = [m_tok, psN_free[nbk], v_tok[b]]
                    if c > 0:
                        wait_l.append(s_copy_tok[hg])
                    P.waits("tensor", wait_l)
                    o_num = psN[nbk][:, j * 65:j * 65 + 64]
                    o_den = psN[nbk][:, j * 65 + 64:j * 65 + 65]
                    v_sl = vtm[:, i * D + h * 64:i * D + h * 64 + 64]
                    sc0 = (h // 2) * 64
                    slot = [("am", o_num, v_sl), ("am", o_den, ones_c[:, :])]
                    if c > 0:
                        slot.append(("q", o_num, S_sb[p0:p0 + 64, sc0:sc0 + 64]))
                        slot.append(("q", o_den, Z_sb[p0:p0 + 64, h // 2:h // 2 + 1]))
                    for si, (kind, o_ap, r_ap) in enumerate(slot):
                        is_first = (j == 0 and si == 0)
                        is_last = (j == 3 and si == len(slot) - 1)
                        if kind == "am":
                            fn = (lambda e, o=o_ap, r=r_ap, abk=abk, s=is_first,
                                  p=is_last: e.matmul(o, am[abk][:, :], r,
                                                      start=s, stop=p))
                        else:
                            fn = (lambda e, o=o_ap, r=r_ap, q=q_sl, s=is_first,
                                  p=is_last: e.matmul(o, q, r, start=s, stop=p))
                        if is_last or si == len(slot) - 1:
                            tok2 = op("tensor", mm, fn)
                            if is_last:
                                num_tok = tok2
                            am_free[abk] = tok2
                        else:
                            P.emit("tensor", fn)
                P.waits("vector", [num_tok, e_toks.get((c - 1, hg))])
                den_src = bass.AP(psN[nbk], 64, [[512, 128], [65, 4]])
                d_tok = op("vector", vv, lambda e, src=den_src, hg=hg: e.tensor_scalar(
                    den16[:, hg * 4:hg * 4 + 4], src, EPSF, None, ALU.max))
                P.wait("scalar", d_tok)
                ssync()
                t_ln = op("scalar", ss, lambda e, hg=hg: e.activation(
                    nsc[:, 16:20], den16[:, hg * 4:hg * 4 + 4], ACT.Ln))
                P.wait("scalar", t_ln)
                op("scalar", ss, lambda e, hg=hg: e.activation(
                    rden16[:, hg * 4:hg * 4 + 4], nsc[:, 16:20], ACT.Exp,
                    scale=-1.0))
                r_tok2 = (ss, ss.n)
                ev = None
                P.wait("vector", r_tok2)
                for j in range(4):
                    h = hg * 4 + j
                    ev = op("vector", vv, lambda e, nbk=nbk, j=j, h=h, i=i, hg=hg:
                            e.tensor_scalar(attn[:, i * D + h * 64:i * D + h * 64 + 64],
                                            psN[nbk][:, j * 65:j * 65 + 64],
                                            rden16[:, hg * 4 + j:hg * 4 + j + 1],
                                            None, ALU.mult))
                psN_free[nbk] = ev
                e_toks[(c, hg)] = ev
                t_attnT_wait.append(ev)

            P.waits("tensor", [v_tok[b], t_ktm, ds_free[0]])
            up_toks_all = []
            for hg in range(4):
                dbank = psD[hg % 2]
                cb = (hg // 2) * 256
                for j in range(4):
                    h = hg * 4 + j
                    p0 = (j % 2) * 64
                    ksl = ktm[:, i * D + h * 64:i * D + h * 64 + 64]
                    vsl = vtm[:, i * D + h * 64:i * D + h * 64 + 64]
                    P.emit("tensor", lambda e, p0=p0, j=j, k=ksl, v=vsl,
                           dbank=dbank, cb=cb:
                           e.matmul(dbank[p0:p0 + 64,
                                          cb + (j // 2) * 64:cb + (j // 2) * 64 + 64],
                                    k, v, start=True, stop=True))
                for j in range(4):
                    h = hg * 4 + j
                    p0 = (j % 2) * 64
                    ksl = ktm[:, i * D + h * 64:i * D + h * 64 + 64]
                    up = op("tensor", mm, lambda e, p0=p0, j=j, k=ksl,
                            dbank=dbank, cb=cb: e.matmul(
                                dbank[p0:p0 + 64,
                                      cb + 128 + j // 2:cb + 129 + j // 2],
                                k, ones_c[:, :],
                                start=True, stop=True))
                up_toks_all.append(up)


            P.waits("vector", up_toks_all + [(mm, mm.n)]
                    + ([t_sfz] if c == 0 else []))
            if debug_taps and c in (0, 1):
                t_dd = op("vector", vv, lambda e: e.tensor_scalar(
                    scr[:, 0:130], psD[0][:, 0:130], 0.0, 0.0,
                    ALU.add, ALU.add))
                tap(f"dbank{c}", scr[:, 0:130], [128, 130], F32, [t_dd])
            for hg in range(4):
                dbank = psD[hg % 2]
                cb = (hg // 2) * 256
                op("vector", vv, lambda e, hg=hg, dbank=dbank, cb=cb:
                   e.tensor_tensor(
                       S_f32[:, hg * 128:hg * 128 + 128],
                       dbank[:, cb:cb + 128],
                       S_f32[:, hg * 128:hg * 128 + 128], ALU.add))
                op("vector", vv, lambda e, hg=hg, dbank=dbank, cb=cb:
                   e.tensor_tensor(
                       Z_f32[:, hg * 2:hg * 2 + 2],
                       dbank[:, cb + 128:cb + 130],
                       Z_f32[:, hg * 2:hg * 2 + 2], ALU.add))
            vsync()
            for hg in range(4):
                op("vector", vv, lambda e, hg=hg: e.tensor_scalar(
                    S_sb[:, hg * 128:hg * 128 + 128],
                    S_f32[:, hg * 128:hg * 128 + 128], 0.0, 0.0,
                    ALU.add, ALU.add))
                s_copy_tok[hg] = op("vector", vv, lambda e, hg=hg: e.tensor_scalar(
                    Z_sb[:, hg * 2:hg * 2 + 2], Z_f32[:, hg * 2:hg * 2 + 2],
                    0.0, 0.0, ALU.add, ALU.add))
            ds_free[0] = s_copy_tok[3]

            if debug_taps and c == 0:
                dbg_gate.append(tap("den16c0", den16[:, :], [128, 16], F32,
                                    list(t_attnT_wait)))
                dbg_gate.append(tap("rden16c0", rden16[:, :], [128, 16], F32,
                                    list(t_attnT_wait)))
                dbg_gate.append(tap("attnc0", attn[:, 0:D], [128, D], BF16,
                                    list(t_attnT_wait)))

            aT_r = attnT.ap().rearrange("p (fc t) -> p fc t", t=BT)
            aT_toks = []
            for grp in range(2):
                pb = nproj[0] % 2
                nproj[0] += 1
                P.waits("tensor", t_attnT_wait + [psP_free[pb], t_id])
                for j in range(4):
                    blk = grp * 4 + j
                    if j < 3:
                        P.emit("tensor", lambda e, pb=pb, j=j, blk=blk, i=i:
                               e.transpose(psP[pb][:, j * 128:(j + 1) * 128],
                                           attn[:, i * D + blk * 128:i * D + blk * 128 + 128],
                                           identf[:, :]))
                tok_t = op("tensor", mm, lambda e, pb=pb, blk=grp * 4 + 3, i=i:
                           e.transpose(psP[pb][:, 384:512],
                                       attn[:, i * D + blk * 128:i * D + blk * 128 + 128],
                                       identf[:, :]))
                P.wait("vector", tok_t)
                ev_t = op("vector", vv, lambda e, pb=pb, grp=grp, i=i, aT_r=aT_r:
                          e.tensor_scalar(
                              aT_r[:, grp * 4:grp * 4 + 4, i * 128:i * 128 + 128],
                              psP[pb][:, 0:512], 0.0, 0.0, ALU.add, ALU.add))
                psP_free[pb] = ev_t
                aT_toks.append(ev_t)
            aT_done[i] = aT_toks

        if b == 0:
            tap("attn", attn[:, :], [128, TPB * D], BF16,
                [e_toks[(cc, hg)] for cc in range(2) for hg in range(4)])
            tap("den16", den16[:, :], [128, 16], F32,
                [e_toks[(1, hg)] for hg in range(4)])
            tap("rden16", rden16[:, :], [128, 16], F32,
                [e_toks[(1, hg)] for hg in range(4)])
            tap("S_sb", S_sb[:, :], [128, 512], BF16, list(s_copy_tok))
            tap("Z_sb", Z_sb[:, :], [128, 8], BF16, list(s_copy_tok))
            tap("psS", S_f32[:, :], [128, 512], F32, list(s_copy_tok))
            tap("psZ", Z_f32[:, :], [128, 8], F32, list(s_copy_tok))

        for i in range(TPB):
            tt = tt0 + i
            for nb_ in range(2):
                lhsTs = [attnT[:, fc * BT + i * 128:fc * BT + i * 128 + 128]
                         for fc in range(8)]
                rhss = [woutT3[:, fc, nb_ * 4:nb_ * 4 + 4, :]
                        for fc in range(8)]
                bank, tok, bi = proj_group(
                    512, lhsTs, rhss,
                    ones_r[0:1, 0:128], outb_r[0:1, nb_ * 512:nb_ * 512 + 512],
                    aT_done[i] + [t_outb, t_wo])
                P.wait("vector", tok)
                r_tok[b] = op("vector", vv, lambda e, bank=bank, tt=tt, nb_=nb_:
                              e.tensor_tensor(x1[:, tt * D + nb_ * 512:tt * D + nb_ * 512 + 512],
                                              bank[:, 0:512],
                                              x1[:, tt * D + nb_ * 512:tt * D + nb_ * 512 + 512],
                                              ALU.add))
                psP_free[bi] = r_tok[b]

        if b == 0:
            tap("x1b0", x1[:, 0:TPB * D], [128, TPB * D], F32, [r_tok[0]])

        P.wait("vector", r_tok[b])
        for tt in range(tt0, tt0 + TPB):
            xs_ap = x1[:, tt * D:(tt + 1) * D]
            vsync()
            op("vector", vv, lambda e, a=xs_ap, c=tt: e.scalar_tensor_tensor(
                scr[:, :], a, 1.0, a, ALU.mult, ALU.mult,
                accum_out=sq2[:, c:c + 1]))
            t_st2 = op("vector", vv, lambda e, a=xs_ap, c=tt: e.tensor_reduce(
                sum2[:, c:c + 1], a, AX, ALU.add))

        for tt in range(tt0, tt0 + TPB):
            vsync()
            op("vector", vv, lambda e, tt=tt: e.tensor_scalar(
                scr[:, :], x1[:, tt * D:(tt + 1) * D],
                float(D), sum2[:, tt:tt + 1],
                ALU.mult, ALU.subtract))
            vsync()
            op("vector", vv, lambda e, tt=tt: e.tensor_reduce(
                hmax[:, tt:tt + 1], scr[:, :], AX, ALU.max,
                apply_absolute_value=True))

    t_rstd2 = emit_ln_finalize(sum2, sq2, mean2, var2, rstd2, t_st2)
    P.wait("vector", t_rstd2)
    op("vector", vv, lambda e: e.tensor_tensor(
        hmax[:, :], rstd2[:, :], hmax[:, :], ALU.mult))

    vsync()
    t_hm = op("vector", vv, lambda e: e.tensor_reduce(
        hmaxc[:, :], hmax[:, :], AX, ALU.max))
    P.wait("gpsimd", t_hm)
    t_cc1 = dma("gpsimd", bass.AP(cc_in, 0, [[1, 128], [1, 1]]), hmaxc[:, :], ld)
    P.wait("gpsimd", t_cc1)
    tok_cc = (cs, cs.bump(1))
    P.emit("gpsimd", lambda e: e.collective_compute(
        "AllReduce", ALU.max, replica_groups=[list(range(8))],
        ins=[cc_in.ap().opt()], outs=[cc_out.ap().opt()]).then_inc(cs.h, 1))
    P.wait("gpsimd", tok_cc)
    t_cc2 = dma("gpsimd", scr[:, 0:128],
                bass.AP(cc_out, 0, [[0, 128], [1, 128]]), ld)
    P.wait("vector", t_cc2)
    for j in range(16):
        op("vector", vv, lambda e, j=j: e.tensor_reduce(
            nsc[:, j:j + 1], scr[:, 0:128], AX, ALU.max))
    vsync()
    t_xs0 = op("vector", vv, lambda e: e.tensor_scalar(
        xsw[:, :], nsc[:, 0:16], 1.0 / (127.0 * D), 1e-5, ALU.mult, ALU.max))
    P.waits("scalar", [t_xs0, t_rstd2])
    ssync()
    t_xl = op("scalar", ss, lambda e: e.activation(
        nsc[:, 0:16], xsw[:, :], ACT.Ln))
    P.wait("scalar", t_xl)
    t_xs = op("scalar", ss, lambda e: e.activation(
        invw[:, :], nsc[:, 0:16], ACT.Exp, scale=-1.0))
    P.waits("vector", [t_xs, t_bffn])
    t_bf = op("vector", vv, lambda e: e.tensor_scalar(
        bffn_i[:, :], bffn_r[:, :], invw[0:1, 0:1], None, ALU.mult))
    t_r2s = op("vector", vv, lambda e: e.tensor_scalar(
        rstd2s[:, :], rstd2[:, :], invw[:, 0:1], None, ALU.mult))

    tap("mean1", mean1[:, :], [128, NT], F32, [t_r2s])
    tap("rstd1", rstd1[:, :], [128, NT], F32, [t_r2s])
    tap("mean2", mean2[:, :], [128, NT], F32, [t_r2s])
    tap("rstd2", rstd2[:, :], [128, NT], F32, [t_r2s])
    tap("hmax", hmax[:, :], [128, NT], F32, [t_r2s])
    tap("xsw", xsw[:, :], [128, 16], F32, [t_r2s])
    tap("invw", invw[:, :], [128, 16], F32, [t_r2s])
    tap("bffn_i", bffn_i[:, :], [1, D], BF16, [t_bf])

    h1ctx.close()
    x8T = [sb(f"x8Tp{i}", [128, D], BF16) for i in range(2)]
    st_tok = None
    scr_free = None
    x8T_free = [None, None]
    o_tok = None
    for tt in range(NT):
        pbuf = tt % 2
        P.waits("vector", [t_rstd2, scr_free])
        vsync()
        t_scr = op("vector", vv, lambda e, tt=tt: e.tensor_scalar(
            scr[:, :], x1[:, tt * D:(tt + 1) * D],
            mean2[:, tt:tt + 1], rstd2[:, tt:tt + 1],
            ALU.subtract, ALU.mult))
        for grp in range(2):
            pb = nproj[0] % 2
            nproj[0] += 1
            P.waits("tensor", [t_scr, psP_free[pb], t_id, x8T_free[pbuf]])
            for j in range(4):
                blk = grp * 4 + j
                if j < 3:
                    P.emit("tensor", lambda e, pb=pb, j=j, blk=blk:
                           e.transpose(psP[pb][:, j * 128:(j + 1) * 128],
                                       scr[:, blk * 128:blk * 128 + 128],
                                       identf[:, :]))
            scr_free = op("tensor", mm, lambda e, pb=pb, blk=grp * 4 + 3:
                          e.transpose(psP[pb][:, 384:512],
                                      scr[:, blk * 128:blk * 128 + 128],
                                      identf[:, :]))
            P.waits("vector", [scr_free, t_xs])
            ev1 = op("vector", vv, lambda e, pb=pb, c0=grp * 512:
                     e.tensor_scalar(attn[:, c0:c0 + 512],
                                     psP[pb][:, 0:512],
                                     invw[:, 0:1], MAGIC, ALU.mult, ALU.add))
            psP_free[pb] = ev1
            vsync()
            ev = op("vector", vv, lambda e, pbuf=pbuf, c0=grp * 512:
                    e.tensor_scalar(x8T[pbuf][:, c0:c0 + 512],
                                    attn[:, c0:c0 + 512],
                                    MAGIC, None, ALU.subtract))
            x8T_ready = ev
        for nb_ in range(2):
            lhsTs = [x8T[pbuf][:, dc * 128:dc * 128 + 128]
                     for dc in range(8)]
            rhss = [wffnT3[:, dc, nb_ * 4:nb_ * 4 + 4, :]
                    for dc in range(8)]
            bank, tok, bi = proj_group(
                512, lhsTs, rhss,
                ones_r[0:1, 0:128], bffn_i[0:1, nb_ * 512:nb_ * 512 + 512],
                [x8T_ready, t_wf, t_bf])
            x8T_free[pbuf] = tok
            P.waits("vector", [tok, t_xs])
            o_tok = op("vector", vv, lambda e, bank=bank, tt=tt, nb_=nb_:
                       e.scalar_tensor_tensor(
                           x1[:, tt * D + nb_ * 512:tt * D + nb_ * 512 + 512],
                           bank[:, 0:512], xsw[:, 0:1],
                           x1[:, tt * D + nb_ * 512:tt * D + nb_ * 512 + 512],
                           ALU.mult, ALU.add))
            psP_free[bi] = o_tok
        if tt % 2 == 1:
            P.wait("gpsimd", o_tok)
            out_r = out_d.ap().rearrange("(a p) d -> p a d", p=128)
            st_tok = dma("gpsimd", out_r[:, tt - 1:tt + 1, :],
                         x1_r[:, tt - 1:tt + 1, :], st)

    P.wait("gpsimd", (st, st.n))

    with ctx, nc.Block() as block:
        @block.gpsimd
        def _(e):
            for f in P.ops["gpsimd"]:
                f(e)

        @block.sync
        def _(e):
            for f in P.ops["sync"]:
                f(e)

        @block.tensor
        def _(e):
            for f in P.ops["tensor"]:
                f(e)

        @block.vector
        def _(e):
            for f in P.ops["vector"]:
                f(e)

        @block.scalar
        def _(e):
            for f in P.ops["scalar"]:
                f(e)

    return nc


_NC_CACHE = None


def kernel(L0_attn_x, L0_attn_qkv_w, L0_attn_qkv_b, L0_attn_out_w,
           L0_attn_out_b, L0_ffn_weight_int8, L0_ffn_weight_scale,
           L0_ffn_bias):
    global _NC_CACHE
    if _NC_CACHE is None:
        _NC_CACHE = build_nc()
    nc = _NC_CACHE
    in_maps = []
    for b in range(8):
        in_maps.append({
            "x": np.ascontiguousarray(L0_attn_x[b]),
            "qkvw": np.ascontiguousarray(L0_attn_qkv_w),
            "qkvb": np.ascontiguousarray(L0_attn_qkv_b.reshape(1, -1)),
            "outw": np.ascontiguousarray(L0_attn_out_w),
            "outb": np.ascontiguousarray(L0_attn_out_b.reshape(1, -1)),
            "w8": np.ascontiguousarray(L0_ffn_weight_int8),
            "ws": np.ascontiguousarray(L0_ffn_weight_scale),
            "bffn": np.ascontiguousarray(L0_ffn_bias.reshape(1, -1)),
        })
    res = run_bass_kernel_spmd(nc, in_maps, core_ids=list(range(8)))
    return np.stack([r["out"] for r in res.results]).astype(np.float32)
